# version 10
# speedup vs baseline: 1.0526x; 1.0093x over previous
import numpy as np
import ml_dtypes
import concourse.bass as bass
import concourse.mybir as mybir

F32 = mybir.dt.float32
BF16 = mybir.dt.bfloat16
F32R = mybir.dt.float32r
U32 = mybir.dt.uint32
AF = mybir.ActivationFunctionType
ALU = mybir.AluOpType
AX = mybir.AxisListType

NT, TT, T, C = 16, 128, 2048, 1024
NH, HD = 16, 64
LC, NE, CAP = 256, 16, 1280
BIGSLOT = 4096
DIN = 2048
NST = 16
DTR = 64


class Sem:
    def __init__(self, stack, nc, name):
        self.h = stack.enter_context(nc.semaphore(name))
        self.n = 0

    def inc(self, instr, k=1):
        instr.then_inc(self.h, k)
        self.n += k
        return instr


def serial(eng, sem):
    def op(fn, *a, **kw):
        instr = fn(*a, **kw)
        sem.inc(instr)
        eng.wait_ge(sem.h, sem.n)
        return instr
    return op


class Ker:
    def __init__(self, nc, stack, taps=()):
        self.nc, self.stack = nc, stack
        self.taps = set(taps)
        self.tap_ap = {}
        self.d = {}
        self.s = {}

    def sb(self, name, shape, dt=F32, stack=None):
        t = (stack or self.stack).enter_context(self.nc.sbuf_tensor(name, shape, dt))
        self.s[name] = t
        return t

    def ps(self, name, shape, dt=F32, stack=None):
        return (stack or self.stack).enter_context(self.nc.psum_tensor(name, shape, dt))

    def sem(self, name, stack=None):
        return Sem(self.stack, self.nc, name)

    def din(self, name, shape, dt=F32):
        self.d[name] = self.nc.dram_tensor(name, shape, dt, kind="ExternalInput").ap()
        return self.d[name]

    def dout(self, name, shape, dt=F32):
        self.d[name] = self.nc.dram_tensor(name, shape, dt, kind="ExternalOutput").ap()
        return self.d[name]

    def dtmp(self, name, shape, dt=F32):
        self.d[name] = self.nc.dram_tensor(name, shape, dt, kind="Internal").ap()
        return self.d[name]

    def tap(self, name, shape, dt=F32):
        if name in self.taps:
            self.tap_ap[name] = self.dout("tap_" + name, shape, dt)
            return self.tap_ap[name]
        return None

    def fence(self):
        marks = [(s.h, s.n) for s in getattr(self, "marks", []) if s.n > 0]
        if not marks:
            return
        with self.nc.Block() as blk:
            for sec in (blk.sync, blk.scalar, blk.vector, blk.tensor, blk.gpsimd):
                def _f(eng, _m=marks):
                    for h, n in _m:
                        eng.wait_ge(h, n)
                sec(_f)
        self.marks = []


def declare_inputs(k: Ker):
    k.din("x", [T, C])
    k.din("rank", [1, 1])
    k.din("tri", [128, 128])
    k.din("ident128f", [128, 128])
    k.din("ident128b", [128, 128], BF16)
    k.din("ident16f", [16, 16])
    k.din("ident65f", [65, 65])
    k.din("cos2", [128, T], BF16)
    k.din("sinf2", [128, T], BF16)
    k.din("pswap", [128, 128], BF16)
    k.din("WqT", [C, C])
    k.din("WkT", [C, C])
    k.din("WvT", [C, C])
    k.din("gate_wT", [C, NE])
    k.din("w1", [NE, C, 4096], F32R)
    k.din("w2", [NE, 4096, C], F32R)
    k.din("in_projT", [C, 2 * DIN])
    k.din("conv_w", [DIN, 4])
    k.din("conv_b", [1, DIN])
    k.din("x_projT", [DIN, 128])
    k.din("dt_projT", [DTR, DIN])
    k.din("dt_proj_b", [1, DIN])
    k.din("A_log", [DIN, NST])
    k.din("D_param", [1, DIN])
    k.din("out_projT", [DIN, C])
    k.din("l1w", [C, C])
    k.din("l1b", [1, C])
    k.dout("out", [T, C])
    k.dtmp("xn_spill", [T, C], BF16)
    k.dtmp("x_spill", [T, C])
    k.dtmp("y_scratch", [BIGSLOT + 1, C], BF16)
    k.dtmp("cc_in", [128, NE])
    k.dtmp("cc_out", [1024, NE])
    k.dtmp("bc_rows", [32, T], BF16)
    k.dtmp("u_spill", [DIN, T], BF16)
    k.dtmp("oz_spill", [DIN, T], BF16)


def emit_blockA(k: Ker):
    nc = k.nc
    from contextlib import ExitStack
    ast = ExitStack()
    k.astack = ast
    X = k.sb("X", [128, NT * 1024], F32)
    ID128B = k.sb("ID128B", [128, 128], BF16)
    PIDX = k.sb("PIDX", [128, 1], F32)
    PROB = k.sb("PROB", [128, NT], F32)
    SLOT = k.sb("SLOT", [128, NT], F32)
    SLOTU = k.sb("SLOTU", [128, NT], U32)
    k.tab_stack = ExitStack()
    ID65F = k.sb("ID65F", [65, 65], F32, stack=k.tab_stack)
    PSWAP = k.sb("PSWAP", [128, 128], BF16, stack=k.tab_stack)
    COS2 = k.sb("COS2", [128, T], BF16, stack=k.tab_stack)
    SINF2 = k.sb("SINF2", [128, T], BF16, stack=k.tab_stack)
    k.xnt_stack = ExitStack()
    XNT = k.sb("XNT", [128, 8 * T], BF16, stack=k.xnt_stack)
    XN = k.sb("XN", [128, NT * 1024], BF16, stack=ast)
    ID128F = k.sb("ID128F", [128, 128], F32, stack=ast)
    ID16F = k.sb("ID16F", [16, 16], F32, stack=ast)
    GWT = k.sb("GWT", [128, 8 * NE], BF16, stack=ast)
    IOTA16 = k.sb("IOTA16", [128, NE], F32, stack=ast)
    ONECF = k.sb("ONECF", [128, 1], F32, stack=ast)
    ONE8 = k.sb("ONE8", [8, 1], F32, stack=ast)
    S1 = k.sb("S1", [128, NT], F32, stack=ast)
    S2 = k.sb("S2", [128, NT], F32, stack=ast)
    MEAN = k.sb("MEAN", [128, NT], F32, stack=ast)
    RSTD = k.sb("RSTD", [128, NT], F32, stack=ast)
    NEGB = k.sb("NEGB", [128, NT], F32, stack=ast)
    SCR = k.sb("SCR", [128, 2048], F32, stack=ast)
    LOG = k.sb("LOG", [128, NT * NE], F32, stack=ast)
    LMAX = k.sb("LMAX", [128, NT], F32, stack=ast)
    NLMAX = k.sb("NLMAX", [128, NT], F32, stack=ast)
    EIDX = k.sb("EIDX", [128, NT], F32, stack=ast)
    OH = k.sb("OH", [128, NT * NE], F32, stack=ast)
    SCR16 = k.sb("SCR16", [128, NE], F32, stack=ast)
    OHT = k.sb("OHT", [16, T], F32, stack=ast)
    ZER16 = k.sb("ZER16", [16, T], F32, stack=ast)
    LOCT = k.sb("LOCT", [16, T], F32, stack=ast)
    KLOCT = k.sb("KLOCT", [16, T], F32, stack=ast)
    MSKT = k.sb("MSKT", [16, T], F32, stack=ast)
    LOCN = k.sb("LOCN", [128, NT * NE], F32, stack=ast)
    KLOCN = k.sb("KLOCN", [128, NT * NE], F32, stack=ast)
    GLC = k.sb("GLC", [128, NT], F32, stack=ast)
    KLOC = k.sb("KLOC", [128, NT], F32, stack=ast)
    KEEP = k.sb("KEEP", [128, NT], F32, stack=ast)
    SCRN = k.sb("SCRN", [128, NT], F32, stack=ast)
    CNTQ = k.sb("CNTQ", [128, NE], F32, stack=ast)
    AGC = k.sb("AGC", [8, NE], F32, stack=ast)
    RANKC = k.sb("RANKC", [8, 1], F32, stack=ast)
    MSK8 = k.sb("MSK8", [8, NE], F32, stack=ast)
    RNKLT = k.sb("RNKLT", [8, 1], F32, stack=ast)

    sx = k.sem("sx")
    scst = k.sem("scst")
    a1 = k.sem("a1")
    a2 = k.sem("a2")
    pe1 = k.sem("pe1")
    ev1 = k.sem("ev1")
    acc = k.sem("acc")
    asp = k.sem("asp")

    xap = k.d["x"]

    with nc.Block() as blk:
        @blk.sync
        def _(s):
            sx.inc(s.dma_start(
                out=X[:, :],
                in_=bass.AP(xap.tensor, 0, [[1024, 128], [128 * 1024, NT], [1, 1024]])), 16)
            for name, dst in [("ident128b", ID128B),
                              ("ident128f", ID128F), ("cos2", COS2),
                              ("sinf2", SINF2), ("pswap", PSWAP)]:
                scst.inc(s.dma_start(out=dst[:, :], in_=k.d[name]), 16)
            scst.inc(s.dma_start(out=ID16F[:, :], in_=k.d["ident16f"]), 16)
            scst.inc(s.dma_start(out=ID65F[:, :], in_=k.d["ident65f"]), 16)
            scst.inc(s.dma_start(
                out=SCR[:, :8 * NE],
                in_=bass.AP(k.d["gate_wT"].tensor, 0,
                            [[NE, 128], [128 * NE, 8], [1, NE]])), 16)
            scst.inc(s.dma_start(out=RANKC[:, :],
                                 in_=bass.AP(k.d["rank"].tensor, 0, [[0, 8], [1, 1]])), 16)

        @blk.gpsimd
        def _(g):
            g.iota(PIDX[:, :], [[0, 1]], channel_multiplier=1,
                   allow_small_or_imprecise_dtypes=True)
            g.iota(IOTA16[:, :], [[1, NE]], channel_multiplier=0,
                   allow_small_or_imprecise_dtypes=True)
            g.memset(ONECF[:, :], 1.0)
            g.memset(ONE8[:, :], 1.0)
            g.memset(ZER16[:, :], 0.0)
            g.memset(CNTQ[:, :], 0.0)

        @blk.scalar
        def _(sc):
            sc.wait_ge(scst.h, scst.n)
            ssc = k.sem("sscA")
            ssc.inc(sc.copy(GWT[:, :], SCR[:, :8 * NE]))
            sc.wait_ge(ssc.h, ssc.n)
            sc.wait_ge(sx.h, sx.n)
            for i in range(NT):
                if i >= 2:
                    sc.wait_ge(a2.h, i - 1)
                a1.inc(sc.activation(SCR[:, (i % 2) * 1024:(i % 2 + 1) * 1024],
                                     X[:, i * 1024:(i + 1) * 1024], AF.Square))

        @blk.vector
        def _(v):
            v.wait_ge(sx.h, sx.n)
            sv = k.sem("svA")
            for i in range(NT):
                sv.inc(v.reduce_sum(S1[:, i:i + 1], X[:, i * 1024:(i + 1) * 1024],
                                    axis=AX.X))
                v.wait_ge(a1.h, i + 1)
                a2.inc(v.reduce_sum(S2[:, i:i + 1],
                                    SCR[:, (i % 2) * 1024:(i % 2 + 1) * 1024],
                                    axis=AX.X))
            v.wait_ge(sv.h, sv.n)
            v.wait_ge(a2.h, a2.n)
            sop = serial(v, sv)
            sop(v.tensor_scalar, MEAN[:, :], S1[:, :], 1.0 / C, None, ALU.mult)
            sop(v.tensor_mul, SCRN[:, :], MEAN[:, :], MEAN[:, :])
            sop(v.tensor_scalar, RSTD[:, :], S2[:, :], 1.0 / C, None, ALU.mult)
            sop(v.tensor_sub, RSTD[:, :], RSTD[:, :], SCRN[:, :])
            a2.inc(v.tensor_scalar(RSTD[:, :], RSTD[:, :], 1e-5, None, ALU.add))


    with nc.Block() as blk:
        @blk.scalar
        def _(sc):
            a1.inc(sc.sqrt(S2[:, :], RSTD[:, :]))

        @blk.vector
        def _(v):
            v.wait_ge(a1.h, a1.n)
            sv2 = k.sem("svB")
            sop = serial(v, sv2)
            sop(v.reciprocal, RSTD[:, :], S2[:, :])
            sop(v.tensor_mul, NEGB[:, :], MEAN[:, :], RSTD[:, :])
            sop(v.tensor_scalar, NEGB[:, :], NEGB[:, :], -1.0, None, ALU.mult)
            for i in range(NT):
                v.tensor_scalar(XN[:, i * 1024:(i + 1) * 1024],
                                X[:, i * 1024:(i + 1) * 1024],
                                RSTD[:, i:i + 1], NEGB[:, i:i + 1],
                                ALU.mult, ALU.add)

    from contextlib import ExitStack
    ps_stack = ExitStack()
    pstA = k.ps("pstA", [128, 512], BF16, stack=ps_stack)
    pstB = k.ps("pstB", [128, 512], BF16, stack=ps_stack)
    with nc.Block() as blk:
        @blk.tensor
        def _(t):
            n = 0
            for j in range(8):
                for ig in range(4):
                    ps = pstA if n % 2 == 0 else pstB
                    if n >= 2:
                        t.wait_ge(ev1.h, n - 1)
                    last = None
                    for q in range(4):
                        i = ig * 4 + q
                        last = t.transpose(
                            ps[:, q * 128:(q + 1) * 128],
                            XN[:, i * 1024 + j * 128: i * 1024 + (j + 1) * 128],
                            ID128B[:, :])
                    pe1.inc(last)
                    n += 1

        @blk.scalar
        def _(sc):
            for n in range(32):
                j, ig = divmod(n, 4)
                ps = pstA if n % 2 == 0 else pstB
                sc.wait_ge(pe1.h, n + 1)
                ev1.inc(sc.copy(XNT[:, j * T + ig * 512: j * T + (ig + 1) * 512],
                                ps[:, :]))

    ps_stack.close()
    ps_stack = ExitStack()
    psl = k.ps("psl", [128, NE], F32, stack=ps_stack)
    pslB = k.ps("pslB", [128, NE], F32, stack=ps_stack)
    with nc.Block() as blk:
        pe_b = pe1.n
        ev_b = ev1.n

        @blk.tensor
        def _(t):
            for i in range(NT):
                ps = psl if i % 2 == 0 else pslB
                if i >= 2:
                    t.wait_ge(ev1.h, ev_b + i - 1)
                last = None
                for kk in range(8):
                    last = t.matmul(ps[:, :],
                                    XNT[:, kk * T + i * 128: kk * T + (i + 1) * 128],
                                    GWT[:, kk * NE:(kk + 1) * NE],
                                    start=(kk == 0), stop=(kk == 7))
                pe1.inc(last)

        @blk.scalar
        def _(sc):
            for i in range(NT):
                ps = psl if i % 2 == 0 else pslB
                sc.wait_ge(pe1.h, pe_b + i + 1)
                ev1.inc(sc.copy(LOG[:, i * NE:(i + 1) * NE], ps[:, :]))

    with nc.Block() as blk:
        @blk.vector
        def _(v):
            svr = k.sem("svR")
            for i in range(NT):
                svr.inc(v.reduce_max(LMAX[:, i:i + 1], LOG[:, i * NE:(i + 1) * NE],
                                     axis=AX.X))
            v.wait_ge(svr.h, svr.n)
            a2.inc(v.tensor_scalar(NLMAX[:, :], LMAX[:, :], -1.0, None, ALU.mult))

        @blk.scalar
        def _(sc):
            sc.wait_ge(a2.h, a2.n)
            for i in range(NT):
                a1.inc(sc.activation(OH[:, i * NE:(i + 1) * NE],
                                     LOG[:, i * NE:(i + 1) * NE], AF.Exp,
                                     bias=NLMAX[:, i:i + 1], scale=1.0))

        @blk.vector
        def _(v):
            v.wait_ge(a1.h, a1.n)
            svp = k.sem("svP")
            sop = serial(v, svp)
            for i in range(NT):
                svp.inc(v.reduce_sum(S1[:, i:i + 1], OH[:, i * NE:(i + 1) * NE],
                                     axis=AX.X))
            v.wait_ge(svp.h, svp.n)
            v.reciprocal(PROB[:, :], S1[:, :])
            for i in range(NT):
                sl = slice(i * NE, (i + 1) * NE)
                sop(v.tensor_scalar, OH[:, sl], LOG[:, sl], LMAX[:, i:i + 1], None,
                    ALU.is_equal)
                sop(v.scalar_tensor_tensor, OH[:, sl], OH[:, sl], -65536.0,
                    IOTA16[:, :], ALU.mult, ALU.add)
                sop(v.tensor_reduce, S2[:, i:i + 1], OH[:, sl], axis=AX.X, op=ALU.min)
            sop(v.tensor_scalar, EIDX[:, :], S2[:, :], 65536.0, None, ALU.add)
            for i in range(NT):
                sl = slice(i * NE, (i + 1) * NE)
                sop(v.tensor_scalar, OH[:, sl], IOTA16[:, :], EIDX[:, i:i + 1], None,
                    ALU.is_equal)

    psoh = k.ps("psoh", [16, 128], F32, stack=ps_stack)
    psohB = k.ps("psohB", [16, 128], F32, stack=ps_stack)
    pscnt = k.ps("pscnt", [1, NE], F32, stack=ps_stack)
    with nc.Block() as blk:
        pe_b = pe1.n
        ev_b = ev1.n

        @blk.tensor
        def _(t):
            for i in range(NT):
                ps = psoh if i % 2 == 0 else psohB
                if i >= 2:
                    t.wait_ge(ev1.h, ev_b + i - 1)
                pe1.inc(t.transpose(ps[:, :], OH[:, i * NE:(i + 1) * NE], ID128F[:, :]))
            last = None
            for i in range(NT):
                last = t.matmul(pscnt[:, :], ONECF[:, :], OH[:, i * NE:(i + 1) * NE],
                                start=(i == 0), stop=(i == NT - 1))
            pe1.inc(last)

        @blk.scalar
        def _(sc):
            for i in range(NT):
                ps = psoh if i % 2 == 0 else psohB
                sc.wait_ge(pe1.h, pe_b + i + 1)
                ev1.inc(sc.copy(OHT[:, i * 128:(i + 1) * 128], ps[:, :]))

        @blk.vector
        def _(v):
            v.wait_ge(pe1.h, pe_b + NT + 1)
            a2.inc(v.tensor_copy(CNTQ[0:1, :], pscnt[:, :]))

    with nc.Block() as blk:
        @blk.gpsimd
        def _(g):
            accg = k.sem("accg")
            acc.inc(g.dma_start(out=k.d["cc_in"], in_=CNTQ[:, :]), 16)
            g.wait_ge(acc.h, acc.n)
            accg.inc(g.collective_compute(
                "AllGather", ALU.bypass, replica_groups=[list(range(8))],
                ins=[k.d["cc_in"].opt()], outs=[k.d["cc_out"].opt()]))
            g.wait_ge(accg.h, accg.n)
            acc.inc(g.dma_start(
                out=AGC[:, :],
                in_=bass.AP(k.d["cc_out"].tensor, 0, [[128 * NE, 8], [1, NE]])), 16)
            g.wait_ge(acc.h, acc.n)

    psprefc = k.ps("psprefc", [16, 1], F32, stack=ps_stack)
    with nc.Block() as blk:
        @blk.vector
        def _(v):
            svq = k.sem("svQ")
            sop = serial(v, svq)
            sop(v.tensor_scalar, RNKLT[:, :], PIDX[0:8, :], RANKC[:, :1], None, ALU.is_lt)
            a2.inc(v.tensor_scalar(MSK8[:, :], AGC[:, :], RNKLT[:, :1], None,
                                   ALU.mult))

        @blk.tensor
        def _(t):
            t.wait_ge(a2.h, a2.n)
            pe1.inc(t.matmul(psprefc[:, :], MSK8[:, :], ONE8[:, :],
                             start=True, stop=True))

        @blk.vector
        def _(v):
            v.wait_ge(pe1.h, pe1.n)
            svs = k.sem("svS")
            sop = serial(v, svs)
            sop(v.tensor_tensor_scan, LOCT[:, :], OHT[:, :], ZER16[:, :], 0.0,
                ALU.add, ALU.add)
            sop(v.tensor_scalar, LOCT[:, :], LOCT[:, :], psprefc[:, :1], None, ALU.add)
            sop(v.tensor_scalar, MSKT[:, :], LOCT[:, :], float(CAP) + 0.5, None,
                ALU.is_lt)
            sop(v.tensor_mul, MSKT[:, :], MSKT[:, :], OHT[:, :])
            sop(v.tensor_tensor_scan, KLOCT[:, :], MSKT[:, :], ZER16[:, :], 0.0,
                ALU.add, ALU.add)

    psbk = k.ps("psbk", [128, NE], F32, stack=ps_stack)
    psbkB = k.ps("psbkB", [128, NE], F32, stack=ps_stack)
    with nc.Block() as blk:
        pe_b = pe1.n
        ev_b = ev1.n

        @blk.tensor
        def _(t):
            for n in range(2 * NT):
                which, i = divmod(n, NT)
                src = LOCT if which == 0 else KLOCT
                ps = psbk if n % 2 == 0 else psbkB
                if n >= 2:
                    t.wait_ge(ev1.h, ev_b + n - 1)
                pe1.inc(t.transpose(ps[:, :], src[:, i * 128:(i + 1) * 128], ID16F[:, :]))

        @blk.scalar
        def _(sc):
            for n in range(2 * NT):
                which, i = divmod(n, NT)
                dst = LOCN if which == 0 else KLOCN
                ps = psbk if n % 2 == 0 else psbkB
                sc.wait_ge(pe1.h, pe_b + n + 1)
                ev1.inc(sc.copy(dst[:, i * NE:(i + 1) * NE], ps[:, :]))

    with nc.Block() as blk:
        @blk.vector
        def _(v):
            svt = k.sem("svT")
            sop = serial(v, svt)
            for i in range(NT):
                sl = slice(i * NE, (i + 1) * NE)
                sop(v.tensor_mul, SCR16[:, :], LOCN[:, sl], OH[:, sl])
                sop(v.tensor_reduce, GLC[:, i:i + 1], SCR16[:, :], axis=AX.X,
                    op=ALU.max)
                sop(v.tensor_mul, SCR16[:, :], KLOCN[:, sl], OH[:, sl])
                sop(v.tensor_reduce, KLOC[:, i:i + 1], SCR16[:, :], axis=AX.X,
                    op=ALU.max)
            sop(v.tensor_scalar, KEEP[:, :], GLC[:, :], float(CAP) + 0.5, None,
                ALU.is_lt)
            sop(v.tensor_scalar, SLOT[:, :], EIDX[:, :], float(LC), -1.0, ALU.mult,
                ALU.add)
            sop(v.tensor_add, SLOT[:, :], SLOT[:, :], KLOC[:, :])
            sop(v.tensor_scalar, SCRN[:, :], KEEP[:, :], -300000.0, 300000.0,
                ALU.mult, ALU.add)
            sop(v.tensor_add, SLOT[:, :], SLOT[:, :], SCRN[:, :])
            sop(v.tensor_scalar, SCRN[:, :], KLOC[:, :], float(LC) + 0.5, None,
                ALU.is_ge)
            sop(v.tensor_scalar, SCRN[:, :], SCRN[:, :], 300000.0, None, ALU.mult)
            sop(v.tensor_add, SLOT[:, :], SLOT[:, :], SCRN[:, :])
            sop(v.tensor_scalar, SLOT[:, :], SLOT[:, :], float(BIGSLOT), None, ALU.min)
            sop(v.tensor_copy, SLOTU[:, :], SLOT[:, :])

    with nc.Block() as blk:
        @blk.sync
        def _(s):
            for i in range(NT):
                asp.inc(s.dma_start(out=k.d["xn_spill"][i * TT:(i + 1) * TT, :],
                                    in_=XN[:, i * 1024:(i + 1) * 1024]), 16)
            tapn = k.tap("xn0", [T, C], BF16)
            if tapn is not None:
                for i in range(NT):
                    asp.inc(s.dma_start(out=tapn[i * TT:(i + 1) * TT, :],
                                        in_=XN[:, i * 1024:(i + 1) * 1024]), 16)
            for nm, buf, w in [("logits", LOG, NT * NE), ("eidx", EIDX, NT),
                               ("gloc", GLC, NT), ("kloc", KLOC, NT),
                               ("keep", KEEP, NT), ("slot", SLOT, NT),
                               ("prob", PROB, NT)]:
                tp = k.tap(nm, [128, w])
                if tp is not None:
                    asp.inc(s.dma_start(out=tp, in_=buf[:, :w]), 16)
            s.wait_ge(asp.h, asp.n)
    ps_stack.close()
    k.marks = [sx, scst, a1, a2, pe1, ev1, acc, asp]
    ast.close()
    return None


def emit_blockB(k: Ker):
    nc = k.nc
    k.fence()
    from contextlib import ExitStack
    bs = ExitStack()
    X, XNT = k.s["X"], k.s["XNT"]
    COS2, SINF2, PSWAP, ID65F = k.s["COS2"], k.s["SINF2"], k.s["PSWAP"], k.s["ID65F"]
    WQS = k.sb("WQS", [128, 1024], F32, stack=bs)
    WKS = k.sb("WKS", [128, 1024], F32, stack=bs)
    WVS = k.sb("WVS", [128, 1024], F32, stack=bs)
    WQB = k.sb("WQB", [128, 1024], BF16, stack=bs)
    WKB = k.sb("WKB", [128, 1024], BF16, stack=bs)
    WVB = k.sb("WVB", [128, 1024], BF16, stack=bs)
    QTS = k.sb("QTS", [128, T], BF16, stack=bs)
    KTS = k.sb("KTS", [128, T], BF16, stack=bs)
    QROT = k.sb("QROT", [128, T], BF16, stack=bs)
    KROT = k.sb("KROT", [128, T], BF16, stack=bs)
    VSB = k.sb("VSB", [128, 16 * 130], BF16, stack=bs)
    PTS = k.sb("PTS", [128, 16 * 512], BF16, stack=bs)
    PTS2 = k.sb("PTS2", [128, 16 * 512], BF16, stack=bs)
    OS0 = k.sb("OS0", [65, 512], F32, stack=bs)
    OS1 = k.sb("OS1", [65, 512], F32, stack=bs)
    RN = k.sb("RN", [2, T], F32, stack=bs)
    CH = k.sb("CH", [2, 4], F32, stack=bs)
    CHD = k.dtmp("chd", [2, 1])
    BIASC = k.sb("BIASC", [128, 2], F32, stack=bs)
    ID2C = k.sb("ID2C", [128, 2], BF16, stack=bs)
    RC = k.sb("RC", [128, 8], F32, stack=bs)
    PS1 = k.ps("PS1", [128, 512], F32, stack=bs)
    PS2 = k.ps("PS2", [128, 512], F32, stack=bs)
    PS3 = k.ps("PS3", [128, 512], F32, stack=bs)
    PS4 = k.ps("PS4", [128, 512], F32, stack=bs)
    PS5 = k.ps("PS5", [65, 512], F32, stack=bs)
    PS5b = k.ps("PS5b", [65, 512], F32, stack=bs)
    PS6 = k.ps("PS6", [2, 512], F32, stack=bs)

    sw = k.sem("swB", bs)
    cv = k.sem("cvB", bs)
    pe = k.sem("peB", bs)
    ev = k.sem("evB", bs)
    dv = k.sem("dvB", bs)
    bb = k.sem("bbB", bs)

    VSBv = VSB[:, :].rearrange("p (i h c) -> p i h c", h=2, c=65)

    def wslice(name, g):
        return bass.AP(k.d[name].tensor, g * 128,
                       [[1024, 128], [128 * 1024, 8], [1, 128]])

    with nc.Block() as blk:
        @blk.sync
        def _(s):
            for nm, dst in [("WqT", WQS), ("WkT", WKS), ("WvT", WVS)]:
                sw.inc(s.dma_start(out=dst[:, :], in_=wslice(nm, 0)), 16)

        @blk.gpsimd
        def _(g):
            g.memset(ID2C[:, :], 0.0)

        @blk.vector
        def _(v):
            v.memset(VSBv[:, :, :, 64:65], 1.0)

    with nc.Block() as blk:
        @blk.gpsimd
        def _(g):
            g.memset(ID2C[0:64, 0:1], 1.0)
            g.memset(ID2C[64:128, 1:2], 1.0)

    for g in range(8):
        par = g % 2
        with nc.Block() as blk:
            @blk.scalar
            def _(sc):
                sc.wait_ge(sw.h, 48 * (g + 1))
                cv.inc(sc.copy(WQB[:, :], WQS[:, :]))
                cv.inc(sc.copy(WKB[:, :], WKS[:, :]))
                cv.inc(sc.copy(WVB[:, :], WVS[:, :]))

        with nc.Block() as blk:
            @blk.sync
            def _(s):
                if g + 1 < 8:
                    for nm, dst in [("WqT", WQS), ("WkT", WKS), ("WvT", WVS)]:
                        sw.inc(s.dma_start(out=dst[:, :], in_=wslice(nm, g + 1)), 16)

        with nc.Block() as blk:
            pe_b, ev_b, dv_b = pe.n, ev.n, dv.n

            @blk.tensor
            def _(t):
                for mat, wb in [(0, WQB), (1, WKB)]:
                    for n in range(4):
                        ps = PS1 if n % 2 == 0 else PS2
                        if mat * 4 + n >= 2:
                            t.wait_ge(ev.h, ev_b + mat * 4 + n - 1)
                        last = None
                        for kk in range(8):
                            last = t.matmul(
                                ps[:, :], wb[:, kk * 128:(kk + 1) * 128],
                                XNT[:, kk * T + n * 512: kk * T + (n + 1) * 512],
                                start=(kk == 0), stop=(kk == 7))
                        pe.inc(last)
                t.wait_ge(ev.h, ev_b + 8)
                for mat, src in [(0, QTS), (1, KTS)]:
                    for n in range(4):
                        ps = PS1 if n % 2 == 0 else PS2
                        if mat * 4 + n >= 2:
                            t.wait_ge(ev.h, ev_b + 8 + mat * 4 + n - 1)
                        pe.inc(t.matmul(ps[:, :], PSWAP[:, :],
                                        src[:, n * 512:(n + 1) * 512],
                                        start=True, stop=True))
                t.wait_ge(dv.h, dv_b + 3)
                for n in range(4):
                    if n >= 1:
                        t.wait_ge(ev.h, ev_b + 16 + n)
                    pe.inc(t.matmul(PS6[:, :], ID2C[:, :],
                                    QROT[:, n * 512:(n + 1) * 512],
                                    start=True, stop=True))
                t.wait_ge(dv.h, dv_b + 5)
                for n in range(4):
                    t.wait_ge(ev.h, ev_b + 20 + n)
                    pe.inc(t.matmul(PS6[:, :], ID2C[:, :],
                                    QROT[:, n * 512:(n + 1) * 512],
                                    start=True, stop=True))
                for i in range(16):
                    ps = PS3 if i % 2 == 0 else PS4
                    if i >= 2:
                        t.wait_ge(ev.h, ev_b + 24 + i - 1)
                    last = None
                    for kk in range(8):
                        last = t.matmul(
                            ps[:, :128],
                            XNT[:, kk * T + i * 128: kk * T + (i + 1) * 128],
                            WVB[:, kk * 128:(kk + 1) * 128],
                            start=(kk == 0), stop=(kk == 7))
                    pe.inc(last)

            @blk.scalar
            def _(sc):
                for mat, dst in [(0, QTS), (1, KTS)]:
                    for n in range(4):
                        ps = PS1 if n % 2 == 0 else PS2
                        sc.wait_ge(pe.h, pe_b + mat * 4 + n + 1)
                        ev.inc(sc.copy(dst[:, n * 512:(n + 1) * 512], ps[:, :]))
                for mat, dst in [(0, QROT), (1, KROT)]:
                    for n in range(4):
                        ps = PS1 if n % 2 == 0 else PS2
                        sc.wait_ge(pe.h, pe_b + 8 + mat * 4 + n + 1)
                        ev.inc(sc.copy(dst[:, n * 512:(n + 1) * 512], ps[:, :]))
                for mat in range(2):
                    for n in range(4):
                        sc.wait_ge(pe.h, pe_b + 16 + mat * 4 + n + 1)
                        if mat == 1 and n == 0:
                            sc.wait_ge(dv.h, dv_b + 5)
                        ev.inc(sc.copy(RN[:, n * 512:(n + 1) * 512], PS6[:, :]))
                for i in range(16):
                    ps = PS3 if i % 2 == 0 else PS4
                    sc.wait_ge(pe.h, pe_b + 24 + i + 1)
                    ev.inc(sc.copy(VSBv[:, i, :, 0:64],
                                   ps[:, :128].rearrange("p (h c) -> p h c", h=2)))

            @blk.vector
            def _(v):
                def dop(fn, *a, **kw):
                    instr = fn(*a, **kw)
                    dv.inc(instr)
                    v.wait_ge(dv.h, dv.n)
                    return instr
                sv = k.sem(f"svBB{g}", bs) if g < 8 else None
                sop = serial(v, sv)
                v.wait_ge(ev.h, ev_b + 16)
                v.wait_ge(pe.h, pe_b + 16)
                sop(v.tensor_mul, QTS[:, :], QTS[:, :], COS2[:, :])
                sop(v.tensor_mul, QROT[:, :], QROT[:, :], SINF2[:, :])
                dop(v.tensor_add, QTS[:, :], QTS[:, :], QROT[:, :])
                sop(v.tensor_mul, KTS[:, :], KTS[:, :], COS2[:, :])
                sop(v.tensor_mul, KROT[:, :], KROT[:, :], SINF2[:, :])
                dop(v.tensor_add, KTS[:, :], KTS[:, :], KROT[:, :])
                dop(v.tensor_mul, QROT[:, :], QTS[:, :], QTS[:, :])
                v.wait_ge(pe.h, pe_b + 20)
                dop(v.tensor_mul, QROT[:, :], KTS[:, :], KTS[:, :])
                v.wait_ge(ev.h, ev_b + 20)
                dop(v.tensor_reduce, CH[:, 0:1], RN[:, 0:T], axis=AX.X, op=ALU.max)
                v.wait_ge(ev.h, ev_b + 24)
                sop(v.tensor_reduce, CH[:, 1:2], RN[:, 0:T], axis=AX.X, op=ALU.max)
                sop(v.tensor_add, CH[:, 2:3], CH[:, 0:1], CH[:, 1:2])
                dop(v.tensor_scalar, CH[:, 3:4], CH[:, 2:3], -0.0625, None, ALU.mult)

            @blk.gpsimd
            def _(gp):
                gp.wait_ge(dv.h, dv_b + 6)
                bb.inc(gp.dma_start(out=CHD, in_=CH[:, 3:4]), 16)
                gp.wait_ge(bb.h, bb.n)
                bb.inc(gp.dma_start(
                    out=BIASC[:, 0:1],
                    in_=bass.AP(CHD.tensor, 0, [[0, 128], [1, 1]])), 16)
                bb.inc(gp.dma_start(
                    out=BIASC[:, 1:2],
                    in_=bass.AP(CHD.tensor, 1, [[0, 128], [1, 1]])), 16)
                gp.wait_ge(bb.h, bb.n)

        with nc.Block() as blk:
            pe_b, ev_b, dv_b = pe.n, ev.n, dv.n
            Smark = {}
            AVend = {}
            Tmark = {}

            def Emark(tqt, h, tk):
                return ev_b + 34 * tqt + 2 * tk + h + 1

            def OSmark(tqt, h):
                return ev_b + 34 * tqt + 33 + h

            def STTmark(tqt, h, i):
                return dv_b + 16 * tqt + 8 * h + 2 * (i + 1)

            @blk.tensor
            def _(t):
                for tqt in range(4):
                    for tk in range(16):
                        for h in range(2):
                            hb = h * 64
                            ps = ([PS1, PS3] if h == 0 else [PS2, PS4])[tk % 2]
                            if tk >= 2:
                                t.wait_ge(ev.h, Emark(tqt, h, tk - 2))
                            elif tqt >= 1:
                                t.wait_ge(ev.h, Emark(tqt - 1, h, 14 + tk))
                            if tk == 1 and tqt >= 1:
                                t.wait_ge(dv.h, STTmark(tqt - 1, 1, 3))
                            pe.inc(t.matmul(
                                ps[:, :], KTS[hb:hb + 64, tk * 128:(tk + 1) * 128],
                                QTS[hb:hb + 64, tqt * 512:(tqt + 1) * 512],
                                start=True, stop=True))
                            Smark[(tqt, h, tk)] = pe.n
                    if tqt >= 1:
                        t.wait_ge(ev.h, OSmark(tqt - 1, 0))
                        t.wait_ge(ev.h, OSmark(tqt - 1, 1))
                    for tk in range(16):
                        for h in range(2):
                            pb = PTS if h == 0 else PTS2
                            t.wait_ge(ev.h, Emark(tqt, h, tk))
                            pe.inc(t.matmul(
                                (PS5 if h == 0 else PS5b)[:, :],
                                VSBv[:, tk, h, :], pb[:, tk * 512:(tk + 1) * 512],
                                start=(tk == 0), stop=(tk == 15)))
                    AVend[tqt] = pe.n
                    for h in range(2):
                        t.wait_ge(ev.h, OSmark(tqt, h))
                        osb = OS0 if (2 * tqt + h) % 2 == 0 else OS1
                        for i in range(4):
                            gi = (2 * tqt + h) * 4 + i
                            if gi >= 2:
                                pidx2 = gi - 2
                                pt2, r2 = divmod(pidx2, 8)
                                h2, i2 = divmod(r2, 4)
                                t.wait_ge(dv.h, STTmark(pt2, h2, i2))
                            ps = PS3 if gi % 2 == 0 else PS4
                            pe.inc(t.transpose(ps[:, :65],
                                               osb[:, i * 128:(i + 1) * 128],
                                               ID65F[:, :]))
                            Tmark[(tqt, h, i)] = pe.n

            @blk.scalar
            def _(sc):
                for tqt in range(4):
                    for tk in range(16):
                        for h in range(2):
                            ps = ([PS1, PS3] if h == 0 else [PS2, PS4])[tk % 2]
                            pb = PTS if h == 0 else PTS2
                            sc.wait_ge(pe.h, Smark[(tqt, h, tk)])
                            ev.inc(sc.activation(pb[:, tk * 512:(tk + 1) * 512],
                                                 ps[:, :], AF.Exp,
                                                 bias=BIASC[:, h:h + 1], scale=0.125))
                    sc.wait_ge(pe.h, AVend[tqt])
                    for h in range(2):
                        if 2 * tqt + h >= 2:
                            pidx2 = 2 * tqt + h - 2
                            t2, h2 = divmod(pidx2, 2)
                            sc.wait_ge(pe.h, Tmark[(t2, h2, 3)])
                        osb = OS0 if (2 * tqt + h) % 2 == 0 else OS1
                        ev.inc(sc.copy(osb[:, :], (PS5 if h == 0 else PS5b)[:, :]))

            @blk.vector
            def _(v):
                for tqt in range(4):
                    for h in range(2):
                        head = 2 * g + h
                        for i in range(4):
                            gi = (2 * tqt + h) * 4 + i
                            ps = PS3 if gi % 2 == 0 else PS4
                            v.wait_ge(pe.h, Tmark[(tqt, h, i)])
                            dv.inc(v.reciprocal(RC[:, gi % 8:gi % 8 + 1],
                                                ps[:, 64:65]))
                            v.wait_ge(dv.h, dv.n)
                            tile = tqt * 4 + i
                            xsl = X[:, tile * 1024 + head * 64:
                                    tile * 1024 + (head + 1) * 64]
                            dv.inc(v.scalar_tensor_tensor(
                                xsl, ps[:, 0:64], RC[:, gi % 8:gi % 8 + 1], xsl,
                                ALU.mult, ALU.add))
                            v.wait_ge(dv.h, dv.n)
    k.marks = [sw, cv, pe, ev, dv, bb]
    bs.close()


def emit_blockC(k: Ker):
    nc = k.nc
    k.fence()
    from contextlib import ExitStack
    k.xnt_stack.close()
    k.tab_stack.close()
    cs = ExitStack()
    es = ExitStack()
    X = k.s["X"]
    SLOT, SLOTU, PROB, PIDX = k.s["SLOT"], k.s["SLOTU"], k.s["PROB"], k.s["PIDX"]
    XN = k.sb("XNC", [128, NT * 1024], BF16, stack=es)
    HT = k.sb("HT", [128, 32 * 256], F32R, stack=es)
    DPT = k.sb("DPT", [128, 8 * 512], F32R, stack=es)
    W1C = k.sb("W1C", [128, 2 * 8 * 512], F32R, stack=es)
    W2C = k.sb("W2C", [128, 4 * 1024], F32R, stack=es)
    M0 = k.sb("M0", [128, 512], BF16, stack=es)
    M1 = k.sb("M1", [128, 512], BF16, stack=es)
    YST = k.sb("YST", [128, 2 * 1024], BF16, stack=es)
    IOTA512 = k.sb("IOTA512c", [128, 512], F32, stack=es)
    SCOL = k.sb("SCOL", [128, 2], F32, stack=es)
    PD = [k.ps(f"PD{i}", [128, 512], F32, stack=cs) for i in range(8)]

    sp = k.sem("spC", cs)
    ysA = k.sem("ysA", cs)
    ysB = k.sem("ysB", cs)
    wd1 = [k.sem("wd1a", cs), k.sem("wd1b", cs)]
    wd2 = [k.sem("wd2a", cs), k.sem("wd2b", cs), k.sem("wd2c", cs), k.sem("wd2d", cs)]
    md = k.sem("mdC", cs)
    pe = k.sem("peC", cs)
    ev = k.sem("evC", cs)
    yd = k.sem("ydC", cs)
    dv = k.sem("dvC", cs)

    with nc.Block() as blk:
        @blk.sync
        def _(s):
            sp.inc(s.dma_start(
                out=XN[:, :],
                in_=bass.AP(k.d["xn_spill"].tensor, 0,
                            [[1024, 128], [128 * 1024, NT], [1, 1024]])), 16)
            s.wait_ge(sp.h, sp.n)

        @blk.gpsimd
        def _(g):
            g.iota(IOTA512[:, :], [[1, 512]], channel_multiplier=0,
                   allow_small_or_imprecise_dtypes=True)
            dv.inc(g.memset(YST[0:1, 0:1024], 0.0))
            g.wait_ge(dv.h, dv.n)
            yd.inc(g.dma_start(out=k.d["y_scratch"][BIGSLOT:BIGSLOT + 1, :],
                               in_=YST[0:1, 0:1024]), 16)
            g.wait_ge(yd.h, 16)

    w1wait = {}
    w2wait = {}
    cw1 = [0, 0]
    cw2 = [0, 0, 0, 0]

    def w1_issue(s, pp, Q, gate=None):
        ee, q = divmod(Q, 8)
        e = 2 * pp + ee
        half = Q % 2
        if gate is not None:
            s.wait_ge(pe.h, gate)
        cw1[half] += 1
        w1wait[(pp, Q)] = 16 * cw1[half]
        wd1[half].inc(s.dma_start(
            out=W1C[:, half * 4096:(half + 1) * 4096],
            in_=bass.AP(k.d["w1"].tensor, e * 1024 * 4096 + q * 512,
                        [[4096, 128], [128 * 4096, 8], [1, 512]])), 16)

    def w2_issue(g, pp, Cc, gate=None):
        ee, c = divmod(Cc, 32)
        e = 2 * pp + ee
        half = Cc % 4
        if gate is not None:
            g.wait_ge(pe.h, gate)
        cw2[half] += 1
        w2wait[(pp, Cc)] = 16 * cw2[half]
        wd2[half].inc(g.dma_start(
            out=W2C[:, half * 1024:(half + 1) * 1024],
            in_=bass.AP(k.d["w2"].tensor, e * 4096 * 1024 + c * 128 * 1024,
                        [[1024, 128], [1, 1024]])), 16)

    for p in range(8):
        e0, e1 = 2 * p, 2 * p + 1
        with nc.Block() as blk:
            pe_b, md_b, ev_b = pe.n, md.n, ev.n
            Mmark = {}
            MMend = {}

            @blk.vector
            def _(v):
                sv = k.sem(f"svC{p}", cs)
                sop = serial(v, sv)
                for kt in range(NT):
                    mb = M0 if kt % 2 == 0 else M1
                    if kt >= 2:
                        v.wait_ge(pe.h, pe_b + kt - 1)
                    sop(v.tensor_scalar, SCOL[:, 0:1], SLOT[:, kt:kt + 1],
                        -512.0 * p, None, ALU.add)
                    md.inc(v.tensor_scalar(mb[:, :], IOTA512[:, :], SCOL[:, 0:1],
                                           None, ALU.is_equal))
                    Mmark[kt] = md.n
                    v.wait_ge(md.h, md.n)

            @blk.tensor
            def _(t):
                for kt in range(NT):
                    mb = M0 if kt % 2 == 0 else M1
                    t.wait_ge(md.h, md_b + kt + 1)
                    last = None
                    for m in range(8):
                        last = t.matmul(
                            PD[m][:, :],
                            XN[:, kt * 1024 + m * 128: kt * 1024 + (m + 1) * 128],
                            mb[:, :], start=(kt == 0), stop=(kt == NT - 1))
                    pe.inc(last)
                    MMend[kt] = pe.n

            @blk.scalar
            def _(sc):
                sc.wait_ge(pe.h, pe_b + NT)
                for m in range(8):
                    ev.inc(sc.copy(DPT[:, m * 512:(m + 1) * 512], PD[m][:, :]))

        with nc.Block() as blk:
            pe_b, ev_b, yd_b = pe.n, ev.n, yd.n
            wd1_b = [wd1[0].n, wd1[1].n]
            wd2_b = [s.n for s in wd2]

            def hend(Q):
                return (Q * 4 + 4) if Q < 8 else (64 + (Q - 8) * 4 + 4)

            def yend(Cc):
                return (32 + Cc + 1) if Cc < 32 else (96 + (Cc - 32) + 1)

            def reluev(g2):
                return (g2 + 1) if g2 < 32 else (g2 + 5)

            @blk.sync
            def _(s):
                qs = range(16) if p == 0 else range(2, 16)
                for Q in qs:
                    s_gate = pe_b + hend(Q - 2) if Q >= 2 else None
                    w1_issue(s, p, Q, s_gate)
                if p + 1 < 8:
                    w1_issue(s, p + 1, 0, pe_b + hend(14))
                    w1_issue(s, p + 1, 1, pe_b + hend(15))

            @blk.gpsimd
            def _(g):
                ccs_ = list(range(64) if p == 0 else range(4, 64))
                for Cc in ccs_:
                    g_gate = pe_b + yend(Cc - 4) if Cc >= 4 else None
                    w2_issue(g, p, Cc, g_gate)
                    if Cc == 33:
                        g.wait_ge(ev.h, ev_b + 36)
                        for m in range(2):
                            ysA.inc(g.dma_start(
                                out=k.d["y_scratch"][2 * p * 256 + m * 128:
                                                     2 * p * 256 + (m + 1) * 128, :],
                                in_=YST[:, m * 1024:(m + 1) * 1024]), 16)
                if p + 1 < 8:
                    for Cc2 in range(4):
                        w2_issue(g, p + 1, Cc2, pe_b + yend(60 + Cc2))
                g.wait_ge(ev.h, ev_b + 72)
                e = 2 * p + 1
                for m in range(2):
                    ysB.inc(g.dma_start(
                        out=k.d["y_scratch"][e * 256 + m * 128:
                                             e * 256 + (m + 1) * 128, :],
                        in_=YST[:, m * 1024:(m + 1) * 1024]), 16)
                g.wait_ge(ysB.h, ysB.n)

            @blk.tensor
            def _(t):
                for ee in range(2):
                    for q in range(8):
                        Q = 8 * ee + q
                        half = Q % 2
                        t.wait_ge(wd1[half].h, w1wait[(p, Q)])
                        for m in range(4):
                            gm = 32 * ee + q * 4 + m
                            if gm >= 2:
                                t.wait_ge(ev.h, ev_b + reluev(gm - 2))
                            ps = PD[gm % 2]
                            last = None
                            for kd in range(8):
                                last = t.matmul(
                                    ps[:, :256],
                                    W1C[:, half * 4096 + kd * 512 + m * 128:
                                        half * 4096 + kd * 512 + (m + 1) * 128],
                                    DPT[:, kd * 512 + ee * 256:
                                        kd * 512 + (ee + 1) * 256],
                                    start=(kd == 0), stop=(kd == 7))
                            pe.inc(last)
                    t.wait_ge(ev.h, ev_b + (32 if ee == 0 else 68))
                    if ee == 1:
                        t.wait_ge(ev.h, ev_b + 36)
                    for c in range(32):
                        Cc = 32 * ee + c
                        half = Cc % 4
                        t.wait_ge(wd2[half].h, w2wait[(p, Cc)])
                        last = None
                        for m in range(2):
                            for n in range(2):
                                last = t.matmul(
                                    PD[4 + m * 2 + n][:, :],
                                    HT[:, c * 256 + m * 128: c * 256 + (m + 1) * 128],
                                    W2C[:, half * 1024 + n * 512:
                                        half * 1024 + (n + 1) * 512],
                                    start=(c == 0), stop=(c == 31))
                        pe.inc(last)

            @blk.scalar
            def _(sc):
                for ee in range(2):
                    for gm32 in range(32):
                        gm = 32 * ee + gm32
                        sc.wait_ge(pe.h, pe_b + (gm + 1 if gm < 32 else gm + 33))
                        ev.inc(sc.activation(HT[:, gm32 * 256:(gm32 + 1) * 256],
                                             PD[gm % 2][:, :256], AF.Relu))
                    if ee == 1:
                        pass
                    for m in range(2):
                        for n in range(2):
                            sc.wait_ge(pe.h, pe_b + 64 * (ee + 1))
                            sc.wait_ge(ysB.h, 32 * p)
                            if ee == 1:
                                sc.wait_ge(ysA.h, 32 * (p + 1))
                            ev.inc(sc.copy(YST[:, m * 1024 + n * 512:
                                               m * 1024 + (n + 1) * 512],
                                           PD[4 + m * 2 + n][:, :]))

    es.close()
    ANSB = [k.sb(f"ANS{i}", [128, 1024], BF16, stack=cs) for i in range(4)]
    gsem = [k.sem(f"gs{i}", cs) for i in range(4)]
    with nc.Block() as blk:
        dv_b = dv.n

        @blk.gpsimd
        def _(g):
            for i in range(NT):
                ab = ANSB[i % 4]
                if i >= 4:
                    g.wait_ge(dv.h, dv_b + i - 3)
                gsem[i % 4].inc(g.indirect_dma_start(
                    out=ab[:, :], out_offset=None,
                    in_=k.d["y_scratch"],
                    in_offset=bass.IndirectOffsetOnAxis(ap=SLOTU[:, i:i + 1], axis=0),
                    bounds_check=BIGSLOT + 1), 16)

        @blk.vector
        def _(v):
            for i in range(NT):
                ab = ANSB[i % 4]
                v.wait_ge(gsem[i % 4].h, 16 * (i // 4 + 1))
                dv.inc(v.scalar_tensor_tensor(
                    X[:, i * 1024:(i + 1) * 1024], ab[:, :], PROB[:, i:i + 1],
                    X[:, i * 1024:(i + 1) * 1024], ALU.mult, ALU.add))
    k.marks = [sp, wd1[0], wd1[1], wd2[0], wd2[1], wd2[2], wd2[3], md, pe, ev, yd, dv,
               gsem[0], gsem[1], ysA, ysB]
    cs.close()


def emit_blockD(k: Ker):
    nc = k.nc
    k.fence()
    from contextlib import ExitStack
    k.d2 = ExitStack()
    ds = ExitStack()
    X, ID128B, PIDX = k.s["X"], k.s["ID128B"], k.s["PIDX"]
    XN1T = k.sb("XN1T", [128, 8 * T], BF16, stack=k.d2)
    XN1 = k.sb("XN1", [128, NT * 1024], BF16, stack=ds)
    S1 = k.sb("dS1", [128, NT], F32, stack=ds)
    S2 = k.sb("dS2", [128, NT], F32, stack=ds)
    MEAN = k.sb("dMEAN", [128, NT], F32, stack=ds)
    RSTD = k.sb("dRSTD", [128, NT], F32, stack=ds)
    NEGB = k.sb("dNEGB", [128, NT], F32, stack=ds)
    SCRN = k.sb("dSCRN", [128, NT], F32, stack=ds)
    AM = k.sb("qAM", [128, 32], F32, stack=ds)
    SCL = k.sb("qSCL", [128, 32], F32, stack=ds)
    RCP = k.sb("qRCP", [128, 32], F32, stack=ds)
    QY = k.sb("qY", [128, 1024], F32, stack=ds)
    QA = k.sb("qA", [128, 1024], F32, stack=ds)
    QU = k.sb("qU", [128, 1024], F32, stack=ds)
    WD = k.sb("qWD", [128, 8 * 1024], BF16, stack=ds)
    WST = QA
    XDT = k.sb("qXDT", [128, 8 * T], BF16, stack=ds)
    WDT = k.sb("qWDT", [128, 8 * 1024], BF16, stack=ds)
    L1BB = k.sb("qL1BB", [1, 1024], BF16, stack=ds)
    ONER = k.sb("qONER", [1, 128], BF16, stack=ds)
    PD1 = k.ps("dPD1", [128, 512], F32, stack=ds)
    PD2 = k.ps("dPD2", [128, 512], F32, stack=ds)
    PT1 = k.ps("dPT1", [128, 512], BF16, stack=ds)
    PT2 = k.ps("dPT2", [128, 512], BF16, stack=ds)

    sx = k.sem("sxD", ds)
    a1 = k.sem("a1D", ds)
    a2 = k.sem("a2D", ds)
    pe = k.sem("peD", ds)
    ev = k.sem("evD", ds)
    dvq = k.sem("dvqD", ds)

    with nc.Block() as blk:
        @blk.scalar
        def _(sc):
            for i in range(NT):
                if i >= 2:
                    sc.wait_ge(a2.h, i - 1)
                a1.inc(sc.activation((QY if i % 2 == 0 else QU)[:, :],
                                     X[:, i * 1024:(i + 1) * 1024], AF.Square))

        @blk.vector
        def _(v):
            sv = k.sem("svD", ds)
            for i in range(NT):
                sv.inc(v.reduce_sum(S1[:, i:i + 1], X[:, i * 1024:(i + 1) * 1024],
                                    axis=AX.X))
                v.wait_ge(a1.h, i + 1)
                a2.inc(v.reduce_sum(S2[:, i:i + 1],
                                    (QY if i % 2 == 0 else QU)[:, :], axis=AX.X))
            v.wait_ge(sv.h, sv.n)
            v.wait_ge(a2.h, a2.n)
            sop = serial(v, sv)
            sop(v.tensor_scalar, MEAN[:, :], S1[:, :], 1.0 / C, None, ALU.mult)
            sop(v.tensor_mul, SCRN[:, :], MEAN[:, :], MEAN[:, :])
            sop(v.tensor_scalar, RSTD[:, :], S2[:, :], 1.0 / C, None, ALU.mult)
            sop(v.tensor_sub, RSTD[:, :], RSTD[:, :], SCRN[:, :])
            a2.inc(v.tensor_scalar(RSTD[:, :], RSTD[:, :], 1e-5, None, ALU.add))

    with nc.Block() as blk:
        @blk.scalar
        def _(sc):
            a1.inc(sc.sqrt(S2[:, :], RSTD[:, :]))

        @blk.vector
        def _(v):
            v.wait_ge(a1.h, a1.n)
            sv2 = k.sem("svD2", ds)
            sop = serial(v, sv2)
            sop(v.reciprocal, RSTD[:, :], S2[:, :])
            sop(v.tensor_mul, NEGB[:, :], MEAN[:, :], RSTD[:, :])
            sop(v.tensor_scalar, NEGB[:, :], NEGB[:, :], -1.0, None, ALU.mult)
            for i in range(NT):
                v.tensor_scalar(XN1[:, i * 1024:(i + 1) * 1024],
                                X[:, i * 1024:(i + 1) * 1024],
                                RSTD[:, i:i + 1], NEGB[:, i:i + 1],
                                ALU.mult, ALU.add)

    with nc.Block() as blk:
        pe_b, ev_b = pe.n, ev.n

        @blk.tensor
        def _(t):
            n = 0
            for j in range(8):
                for ig in range(4):
                    ps = PT1 if n % 2 == 0 else PT2
                    if n >= 2:
                        t.wait_ge(ev.h, ev_b + n - 1)
                    last = None
                    for q in range(4):
                        i = ig * 4 + q
                        last = t.transpose(
                            ps[:, q * 128:(q + 1) * 128],
                            XN1[:, i * 1024 + j * 128: i * 1024 + (j + 1) * 128],
                            ID128B[:, :])
                    pe.inc(last)
                    n += 1

        @blk.scalar
        def _(sc):
            for n in range(32):
                j, ig = divmod(n, 4)
                ps = PT1 if n % 2 == 0 else PT2
                sc.wait_ge(pe.h, pe_b + n + 1)
                ev.inc(sc.copy(XN1T[:, j * T + ig * 512: j * T + (ig + 1) * 512],
                               ps[:, :]))

    with nc.Block() as blk:
        @blk.gpsimd
        def _(g):
            g.memset(ONER[:, :], 1.0)

    svQ = k.sem("svQD", ds)
    with nc.Block() as blk:
        sx_b = sx.n

        @blk.sync
        def _(s):
            for wi in range(8):
                s.wait_ge(svQ.h, 16 * (NT + wi))
                sx.inc(s.dma_start(out=WST[:, :],
                                   in_=k.d["l1w"][wi * 128:(wi + 1) * 128, :]), 16)

        @blk.vector
        def _(v):
            sop = serial(v, svQ)
            for i in range(NT + 8):
                is_w = i >= NT
                wi = i - NT
                if is_w:
                    v.wait_ge(sx.h, sx_b + 16 * (wi + 1))
                src_ = WST[:, :] if is_w else XN1[:, i * 1024:(i + 1) * 1024]
                dst_ = (WD[:, wi * 1024:(wi + 1) * 1024] if is_w
                        else XN1[:, i * 1024:(i + 1) * 1024])
                sop(v.tensor_reduce, AM[:, :],
                    src_.rearrange("p (b k) -> p b k", k=32),
                    axis=AX.X, op=ALU.max, apply_absolute_value=True)
                sop(v.tensor_scalar, SCL[:, :], AM[:, :], 1.0 / 7.5, 1e-30,
                    ALU.mult, ALU.max)
                sop(v.reciprocal, RCP[:, :], SCL[:, :])
                sop(v.tensor_tensor,
                    QY[:, :].rearrange("p (b k) -> p b k", k=32),
                    src_.rearrange("p (b k) -> p b k", k=32),
                    bass.AP(RCP, 0, [[32, 128], [1, 32], [0, 32]]), ALU.mult)
                sop(v.tensor_scalar, QA[:, :], QY[:, :], -1.0, None, ALU.mult)
                sop(v.tensor_max, QA[:, :], QA[:, :], QY[:, :])
                sop(v.tensor_scalar, QU[:, :], QA[:, :], 2.0, 1.0, ALU.is_ge, ALU.add)
                sop(v.tensor_scalar, QA[:, :], QA[:, :], 4.0, 1.0, ALU.is_ge, ALU.add)
                sop(v.tensor_mul, QU[:, :], QU[:, :], QA[:, :])
                sop(v.reciprocal, QA[:, :], QU[:, :])
                sop(v.tensor_mul, QY[:, :], QY[:, :], QA[:, :])
                sop(v.tensor_scalar, QY[:, :], QY[:, :], 8.0, 12582912.0,
                    ALU.mult, ALU.add)
                sop(v.tensor_scalar, QY[:, :], QY[:, :], 1.0, -12582912.0,
                    ALU.mult, ALU.add)
                sop(v.tensor_mul, QY[:, :], QY[:, :], QU[:, :])
                sop(v.tensor_scalar, QY[:, :], QY[:, :], 0.125, None, ALU.mult)
                sop(v.tensor_tensor,
                    dst_.rearrange("p (b k) -> p b k", k=32),
                    QY[:, :].rearrange("p (b k) -> p b k", k=32),
                    bass.AP(SCL, 0, [[32, 128], [1, 32], [0, 32]]), ALU.mult)

        pe_b, ev_b = pe.n, ev.n

        @blk.tensor
        def _(t):
            n = 0
            for j in range(8):
                for ig in range(4):
                    ps = PT1 if n % 2 == 0 else PT2
                    if n >= 2:
                        t.wait_ge(ev.h, ev_b + n - 1)
                    last = None
                    for q in range(4):
                        i = ig * 4 + q
                        t.wait_ge(svQ.h, 16 * (i + 1))
                        last = t.transpose(
                            ps[:, q * 128:(q + 1) * 128],
                            XN1[:, i * 1024 + j * 128: i * 1024 + (j + 1) * 128],
                            ID128B[:, :])
                    pe.inc(last)
                    n += 1
            for j in range(8):
                for ig in range(2):
                    ps = PT1 if n % 2 == 0 else PT2
                    if n >= 2:
                        t.wait_ge(ev.h, ev_b + n - 1)
                    last = None
                    for q in range(4):
                        i = ig * 4 + q
                        t.wait_ge(svQ.h, 16 * (NT + i + 1))
                        last = t.transpose(
                            ps[:, q * 128:(q + 1) * 128],
                            WD[:, i * 1024 + j * 128: i * 1024 + (j + 1) * 128],
                            ID128B[:, :])
                    pe.inc(last)
                    n += 1

        @blk.scalar
        def _(sc):
            for n in range(32):
                j, ig = divmod(n, 4)
                ps = PT1 if n % 2 == 0 else PT2
                sc.wait_ge(pe.h, pe_b + n + 1)
                ev.inc(sc.copy(XDT[:, j * T + ig * 512: j * T + (ig + 1) * 512],
                               ps[:, :]))
            for n2 in range(16):
                n = 32 + n2
                j, ig = divmod(n2, 2)
                ps = PT1 if n % 2 == 0 else PT2
                sc.wait_ge(pe.h, pe_b + n + 1)
                ev.inc(sc.copy(WDT[:, j * 1024 + ig * 512: j * 1024 + (ig + 1) * 512],
                               ps[:, :]))

    with nc.Block() as blk:
        @blk.sync
        def _(s):
            sx.inc(s.dma_start(out=WST[0:1, :1024], in_=k.d["l1b"]), 16)
            s.wait_ge(sx.h, sx.n)

    with nc.Block() as blk:
        @blk.scalar
        def _(sc):
            sc.copy(L1BB[:, :], WST[0:1, :1024])

    with nc.Block() as blk:
        pe_b, dv_b = pe.n, 0
        dvr = k.sem("dvrD", ds)

        @blk.tensor
        def _(t):
            for m in range(16):
                for n in range(2):
                    idx = m * 2 + n
                    ps = PD1 if idx % 2 == 0 else PD2
                    if idx >= 2:
                        t.wait_ge(dvr.h, idx - 1)
                    for kk in range(8):
                        t.matmul(ps[:, :],
                                 XDT[:, kk * T + m * 128: kk * T + (m + 1) * 128],
                                 WDT[:, kk * 1024 + n * 512: kk * 1024 + (n + 1) * 512],
                                 start=(kk == 0), stop=False)
                    pe.inc(t.matmul(ps[:, :], ONER[:, :],
                                    L1BB[:, n * 512:(n + 1) * 512],
                                    start=False, stop=True))

        @blk.vector
        def _(v):
            for m in range(16):
                for n in range(2):
                    idx = m * 2 + n
                    ps = PD1 if idx % 2 == 0 else PD2
                    v.wait_ge(pe.h, pe_b + idx + 1)
                    xsl = X[:, m * 1024 + n * 512: m * 1024 + (n + 1) * 512]
                    dvr.inc(v.tensor_add(xsl, xsl, ps[:, :]))
    k.marks = [sx, a1, a2, pe, ev, dvq, dvr, svQ]
    ds.close()


def emit_blockE(k: Ker):
    nc = k.nc
    k.fence()
    from contextlib import ExitStack
    es = ExitStack()
    k.e2 = ExitStack()
    X, XN1T = k.s["X"], k.s["XN1T"]
    DTLB = k.sb("DTLB", [64, T], BF16, stack=k.e2)
    DPB = k.sb("DPB", [64, DIN], BF16, stack=k.e2)
    AA = k.sb("AA", [128, 16 * 16], F32, stack=k.e2)
    DTL = k.sb("DTL", [64, T], F32, stack=es)
    BAC = k.sb("BAC", [16, T], F32, stack=es)
    CAC = k.sb("CAC", [16, T], F32, stack=es)
    IPS = k.sb("IPS", [128, 2 * 8 * 128], F32, stack=es)
    IPB = k.sb("IPB", [128, 8 * 128], BF16, stack=es)
    XC = k.sb("XC", [128, T], F32, stack=es)
    CACC = k.sb("CACC", [128, T], F32, stack=es)
    XPS = k.sb("XPS", [128, 2 * 128], F32, stack=es)
    XPB = k.sb("XPB", [128, 128], BF16, stack=es)
    CW = k.sb("CW", [128, 2 * 4], F32, stack=es)
    CB = k.sb("CB", [128, 2], F32, stack=es)
    DPS = k.sb("DPS", [64, DIN], F32, stack=es)
    ALS = k.sb("ALS", [128, 16 * 16], F32, stack=es)
    UC = k.sb("UC", [128, T], BF16, stack=es)
    BCBB = k.sb("BCBB", [16, T], BF16, stack=es)
    BCBC = k.sb("BCBC", [16, T], BF16, stack=es)
    PE1 = k.ps("ePE1", [128, 512], F32, stack=es)
    PE2 = k.ps("ePE2", [128, 512], F32, stack=es)
    PE3 = k.ps("ePE3", [128, 512], F32, stack=es)
    PE4 = k.ps("ePE4", [128, 512], F32, stack=es)

    sx = k.sem("sxE", es)
    sxp = [k.sem("sxEa", es), k.sem("sxEb", es)]
    pe = k.sem("peE", es)
    ev = k.sem("evE", es)
    dv = k.sem("dvE", es)
    aw = k.sem("awE", es)
    uw = k.sem("uwE", es)
    svE = k.sem("svE", es)
    sscE = k.sem("sscE", es)

    with nc.Block() as blk:
        @blk.sync
        def _(s):
            sx.inc(s.dma_start(out=DPS[:, :], in_=k.d["dt_projT"]), 16)
            sx.inc(s.dma_start(
                out=ALS[:, :],
                in_=bass.AP(k.d["A_log"].tensor, 0,
                            [[16, 128], [128 * 16, 16], [1, 16]])), 16)
            s.wait_ge(sx.h, sx.n)

        @blk.vector
        def _(v):
            v.memset(DTL[:, :], 0.0)
            v.memset(BAC[:, :], 0.0)
            v.memset(CAC[:, :], 0.0)

    with nc.Block() as blk:
        @blk.scalar
        def _(sc):
            sc.copy(DPB[:, :], DPS[:, :])
            sc.activation(AA[:, :], ALS[:, :], AF.Exp)

    with nc.Block() as blk:
        @blk.vector
        def _(v):
            v.tensor_scalar(AA[:, :], AA[:, :], -1.0, None, ALU.mult)

    for ch in range(NT):
        with nc.Block() as blk:
            pe_b, ev_b, dv_b, sx_b = pe.n, ev.n, dv.n, sx.n

            par = ch % 2

            @blk.sync
            def _(s):
                if ch >= 1:
                    s.wait_ge(ev.h, 7 * (ch - 1) + 2)
                for ld in ([ch, ch + 1] if ch == 0 else
                           ([ch + 1] if ch + 1 < NT else [])):
                    lp = ld % 2
                    sxp[lp].inc(s.dma_start(
                        out=IPS[:, lp * 1024:(lp + 1) * 1024],
                        in_=bass.AP(k.d["in_projT"].tensor, ld * 128,
                                    [[4096, 128], [128 * 4096, 8], [1, 128]])), 16)
                    sxp[lp].inc(s.dma_start(
                        out=XPS[:, lp * 128:(lp + 1) * 128],
                        in_=k.d["x_projT"][ld * 128:(ld + 1) * 128, :]), 16)
                    sxp[lp].inc(s.dma_start(
                        out=CW[:, lp * 4:(lp + 1) * 4],
                        in_=k.d["conv_w"][ld * 128:(ld + 1) * 128, :]), 16)
                    sxp[lp].inc(s.dma_start(
                        out=CB[:, lp:lp + 1],
                        in_=bass.AP(k.d["conv_b"].tensor, ld * 128,
                                    [[1, 128], [1, 1]])), 16)

            @blk.scalar
            def _(sc):
                sc.wait_ge(sxp[par].h, 64 * (ch // 2 + 1))
                ev.inc(sc.copy(IPB[:, :], IPS[:, par * 1024:(par + 1) * 1024]))
                ev.inc(sc.copy(XPB[:, :], XPS[:, par * 128:(par + 1) * 128]))
                sc.wait_ge(ev.h, ev_b + 2)
                for n in range(4):
                    sc.wait_ge(pe.h, pe_b + n + 1)
                    ev.inc(sc.copy(XC[:, n * 512:(n + 1) * 512],
                                   (PE1 if n % 2 == 0 else PE2)[:, :]))
                sc.wait_ge(dv.h, dv_b + 1)
                ev.inc(sc.activation(UC[:, :], CACC[:, :],
                                     AF.Silu, bias=CB[:, par:par + 1], scale=1.0))

            @blk.tensor
            def _(t):
                t.wait_ge(ev.h, ev_b + 2)
                for n in range(4):
                    ps = PE1 if n % 2 == 0 else PE2
                    if n >= 2:
                        t.wait_ge(ev.h, ev_b + 2 + n - 1)
                    last = None
                    for kk in range(8):
                        last = t.matmul(
                            ps[:, :], IPB[:, kk * 128:(kk + 1) * 128],
                            XN1T[:, kk * T + n * 512: kk * T + (n + 1) * 512],
                            start=(kk == 0), stop=(kk == 7))
                    pe.inc(last)
                t.wait_ge(ev.h, ev_b + 7)
                for n in range(4):
                    ps = PE3 if n % 2 == 0 else PE4
                    if n >= 2:
                        t.wait_ge(dv.h, dv_b + 2 + n - 2)
                    pe.inc(t.matmul(ps[:, :], XPB[:, :],
                                    UC[:, n * 512:(n + 1) * 512],
                                    start=True, stop=True))

            @blk.gpsimd
            def _(g):
                g.wait_ge(ev.h, ev_b + 7)
                uw.inc(g.dma_start(out=k.d["u_spill"][ch * 128:(ch + 1) * 128, :],
                                   in_=UC[:, :]), 16)
                g.wait_ge(uw.h, uw.n)

            @blk.vector
            def _(v):
                sop = serial(v, svE)

                def dop(fn, *a, **kw):
                    instr = fn(*a, **kw)
                    dv.inc(instr)
                    v.wait_ge(dv.h, dv.n)
                    return instr
                v.wait_ge(ev.h, ev_b + 6)
                cwb = par * 4
                sop(v.tensor_scalar, CACC[:, :], XC[:, :], CW[:, cwb + 3:cwb + 4],
                    None, ALU.mult)
                sop(v.scalar_tensor_tensor, CACC[:, 1:T], XC[:, 0:T - 1],
                    CW[:, cwb + 2:cwb + 3], CACC[:, 1:T], ALU.mult, ALU.add)
                sop(v.scalar_tensor_tensor, CACC[:, 2:T], XC[:, 0:T - 2],
                    CW[:, cwb + 1:cwb + 2], CACC[:, 2:T], ALU.mult, ALU.add)
                dop(v.scalar_tensor_tensor, CACC[:, 3:T], XC[:, 0:T - 3],
                    CW[:, cwb:cwb + 1], CACC[:, 3:T], ALU.mult, ALU.add)
                for n in range(4):
                    ps = PE3 if n % 2 == 0 else PE4
                    v.wait_ge(pe.h, pe_b + 4 + n + 1)
                    sl = slice(n * 512, (n + 1) * 512)
                    sop(v.tensor_add, CAC[:, sl], CAC[:, sl], ps[0:16, :])
                    sop(v.tensor_add, BAC[:, sl], BAC[:, sl], ps[32:48, :])
                    dop(v.tensor_add, DTL[:, sl], DTL[:, sl], ps[64:128, :])

    with nc.Block() as blk:
        @blk.scalar
        def _(sc):
            sc.copy(DTLB[:, :], DTL[:, :])
            sc.copy(BCBB[:, :], BAC[:, :])
            sc.copy(BCBC[:, :], CAC[:, :])

    with nc.Block() as blk:
        @blk.sync
        def _(s):
            aw.inc(s.dma_start(out=k.d["bc_rows"][0:16, :], in_=BCBB[:, :]), 16)
            aw.inc(s.dma_start(out=k.d["bc_rows"][16:32, :], in_=BCBC[:, :]), 16)
            s.wait_ge(aw.h, aw.n)
    k.marks = [sx, sxp[0], sxp[1], pe, ev, dv, aw, uw, svE]
    es.close()


def emit_blockF(k: Ker):
    nc = k.nc
    k.fence()
    from contextlib import ExitStack
    fs = ExitStack()
    X, XN1T = k.s["X"], k.s["XN1T"]
    DTLB, DPB, AA = k.s["DTLB"], k.s["DPB"], k.s["AA"]
    UC2 = k.sb("UC2", [128, T], BF16, stack=fs)
    OZC = UC2
    DELC = k.sb("DELC", [128, T], F32, stack=fs)
    DU = k.sb("DU", [128, T], BF16, stack=fs)
    SZC = k.sb("SZC", [128, T], BF16, stack=fs)
    IPS2 = k.sb("IPS2", [128, 8 * 128], F32, stack=fs)
    IPB2 = k.sb("IPB2", [128, 8 * 128], BF16, stack=fs)
    T1 = k.sb("fT1", [128, 512], F32, stack=fs)
    T2 = k.sb("fT2", [128, 512], F32, stack=fs)
    DA = [k.sb(f"fDA{i}", [128, T], BF16, stack=fs) for i in range(2)]
    DBU = [k.sb(f"fDBU{i}", [128, T], BF16, stack=fs) for i in range(2)]
    HH = [k.sb(f"fH{i}", [128, T], BF16, stack=fs) for i in range(2)]
    TY = [k.sb(f"fTY{i}", [128, T], BF16, stack=fs) for i in range(2)]
    BB = [k.sb(f"fBB{i}", [128, T], BF16, stack=fs) for i in range(2)]
    CC = [k.sb(f"fCC{i}", [128, T], BF16, stack=fs) for i in range(2)]
    YP = k.sb("fYP", [128, 4 * T], BF16, stack=fs)
    YC = k.sb("fYC", [128, T], F32, stack=fs)
    CBD = k.sb("fCBD", [128, 1], F32, stack=fs)
    DCOL = k.sb("fDCOL", [128, 1], F32, stack=fs)
    PF1 = k.ps("fPF1", [128, 512], F32, stack=fs)
    PF2 = k.ps("fPF2", [128, 512], F32, stack=fs)

    sx = k.sem("sxF", fs)
    pe = k.sem("peF", fs)
    ev = k.sem("evF", fs)
    dv = k.sem("dvF", fs)
    po = k.sem("poF", fs)
    yd = k.sem("ydF", fs)
    yds = [k.sem("ydFa", fs), k.sem("ydFb", fs)]
    svF = k.sem("svF", fs)
    bcs = [k.sem("bcF0", fs), k.sem("bcF1", fs)]
    ccs = [k.sem("ccF0", fs), k.sem("ccF1", fs)]

    for ch in range(NT):
        with nc.Block() as blk:
            pe_b, ev_b, dv_b, po_b, yd_b = pe.n, ev.n, dv.n, po.n, yd.n
            yds_b = [yds[0].n, yds[1].n]
            bcs_b = [bcs[0].n, bcs[1].n]
            ccs_b = [ccs[0].n, ccs[1].n]

            @blk.sync
            def _(s):
                sx.inc(s.dma_start(out=UC2[:, :],
                                   in_=k.d["u_spill"][ch * 128:(ch + 1) * 128, :]), 16)
                sx.inc(s.dma_start(
                    out=IPS2[:, :],
                    in_=bass.AP(k.d["in_projT"].tensor, DIN + ch * 128,
                                [[4096, 128], [128 * 4096, 8], [1, 128]])), 16)
                sx.inc(s.dma_start(
                    out=CBD[:, :],
                    in_=bass.AP(k.d["dt_proj_b"].tensor, ch * 128,
                                [[1, 128], [1, 1]])), 16)
                sx.inc(s.dma_start(
                    out=DCOL[:, :],
                    in_=bass.AP(k.d["D_param"].tensor, ch * 128,
                                [[1, 128], [1, 1]])), 16)
                s.wait_ge(sx.h, sx.n)
                for n in range(16):
                    par = n % 2
                    if n >= 2:
                        s.wait_ge(dv.h, dv_b + 6 + 3 * (n - 2))
                    bcs[par].inc(s.dma_start(
                        out=BB[par][:, :],
                        in_=bass.AP(k.d["bc_rows"].tensor, n * T,
                                    [[0, 128], [1, T]])), 16)
                    if n >= 2:
                        s.wait_ge(dv.h, dv_b + 8 + 3 * (n - 2))
                    ccs[par].inc(s.dma_start(
                        out=CC[par][:, :],
                        in_=bass.AP(k.d["bc_rows"].tensor, (16 + n) * T,
                                    [[0, 128], [1, T]])), 16)

            LnM, SiluM, DAM = {}, {}, {}

            @blk.scalar
            def _(sc):
                sc.wait_ge(sx.h, sx.n)
                ev.inc(sc.copy(IPB2[:, :], IPS2[:, :]))
                sc.wait_ge(ev.h, ev_b + 1)
                for n in range(4):
                    ps = PF1 if n % 2 == 0 else PF2
                    sc.wait_ge(pe.h, pe_b + n + 1)
                    ev.inc(sc.activation(T1[:, :], ps[:, :], AF.Abs,
                                         bias=CBD[:, :1], scale=1.0))
                    sc.wait_ge(ev.h, ev.n)
                    ev.inc(sc.activation(T2[:, :], T1[:, :], AF.Exp, scale=-1.0))
                    sc.wait_ge(ev.h, ev.n)
                    ev.inc(sc.activation(T1[:, :], T2[:, :], AF.Ln, bias=1.0))
                    LnM[n] = ev.n
                    sc.wait_ge(dv.h, dv_b + n + 1)
                for n in range(4):
                    ps = PF1 if n % 2 == 0 else PF2
                    sc.wait_ge(pe.h, pe_b + 4 + n + 1)
                    ev.inc(sc.activation(SZC[:, n * 512:(n + 1) * 512], ps[:, :],
                                         AF.Silu))
                    SiluM[n] = ev.n
                sc.wait_ge(dv.h, dv_b + 4)
                for n in range(16):
                    par = n % 2
                    if n >= 2:
                        sc.wait_ge(dv.h, dv_b + 7 + 3 * (n - 2))
                    ev.inc(sc.activation(DA[par][:, :], DELC[:, :], AF.Exp,
                                         scale=AA[:, ch * 16 + n: ch * 16 + n + 1]))
                    DAM[n] = ev.n

            @blk.tensor
            def _(t):
                for n in range(4):
                    ps = PF1 if n % 2 == 0 else PF2
                    if n >= 2:
                        t.wait_ge(dv.h, dv_b + n - 1)
                    pe.inc(t.matmul(ps[:, :], DPB[:, ch * 128:(ch + 1) * 128],
                                    DTLB[:, n * 512:(n + 1) * 512],
                                    start=True, stop=True))
                t.wait_ge(dv.h, dv_b + 4)
                for n in range(4):
                    ps = PF1 if n % 2 == 0 else PF2
                    if n >= 2:
                        t.wait_ge(ev.h, SiluM[n - 2])
                    last = None
                    for kk in range(8):
                        last = t.matmul(
                            ps[:, :], IPB2[:, kk * 128:(kk + 1) * 128],
                            XN1T[:, kk * T + n * 512: kk * T + (n + 1) * 512],
                            start=(kk == 0), stop=(kk == 7))
                    pe.inc(last)

            @blk.vector
            def _(v):
                sop = serial(v, svF)

                def dop(fn, *a, **kw):
                    instr = fn(*a, **kw)
                    dv.inc(instr)
                    v.wait_ge(dv.h, dv.n)
                    return instr
                for n in range(4):
                    ps = PF1 if n % 2 == 0 else PF2
                    sl = slice(n * 512, (n + 1) * 512)
                    v.wait_ge(ev.h, LnM[n])
                    sop(v.tensor_scalar, YC[:, 0:512], ps[:, :], CBD[:, :1], 0.0,
                        ALU.add, ALU.max)
                    dop(v.tensor_add, DELC[:, sl], YC[:, 0:512], T1[:, :])
                dop(v.tensor_tensor, DU[:, :], DELC[:, :], UC2[:, :], ALU.mult)
                for n in range(16):
                    par = n % 2
                    v.wait_ge(bcs[par].h, bcs_b[par] + 16 * (n // 2 + 1))
                    dop(v.tensor_tensor, DBU[par][:, :], DU[:, :], BB[par][:, :],
                        ALU.mult)
                    v.wait_ge(ev.h, DAM[n])
                    dop(v.tensor_tensor_scan, HH[par][:, :], DA[par][:, :],
                        DBU[par][:, :], 0.0, ALU.mult, ALU.add)
                    v.wait_ge(ccs[par].h, ccs_b[par] + 16 * (n // 2 + 1))
                    if n >= 2:
                        v.wait_ge(yds[par].h, yds_b[par] + 16 * (n // 2))
                    dop(v.tensor_tensor, TY[par][:, :], HH[par][:, :],
                        CC[par][:, :], ALU.mult)
                v.wait_ge(yds[0].h, yds_b[0] + 16 * 8)
                v.wait_ge(yds[1].h, yds_b[1] + 16 * 8)
                sop(v.tensor_add, YC[:, :], YP[:, 0:T], YP[:, T:2 * T])
                sop(v.tensor_add, DELC[:, :], YP[:, 2 * T:3 * T], YP[:, 3 * T:4 * T])
                sop(v.tensor_add, YC[:, :], YC[:, :], DELC[:, :])
                sop(v.scalar_tensor_tensor, YC[:, :], UC2[:, :],
                    DCOL[:, :1], YC[:, :], ALU.mult, ALU.add)
                v.wait_ge(ev.h, SiluM[3])
                dop(v.tensor_tensor, OZC[:, :], YC[:, :], SZC[:, :], ALU.mult)

            @blk.gpsimd
            def _(g):
                for n in range(16):
                    par = n % 2
                    g.wait_ge(dv.h, dv_b + 8 + 3 * n)
                    if n >= 4:
                        g.wait_ge(yds[n % 2].h, yds_b[n % 2] + 16 * (n // 2 - 1))
                    tgt = YP[:, (n % 4) * T:(n % 4 + 1) * T]
                    if n < 4:
                        yds[n % 2].inc(g.dma_start(out=tgt, in_=TY[par][:, :]), 16)
                    else:
                        yds[n % 2].inc(g.dma_start(out=tgt, in_=TY[par][:, :],
                                                   accum_op=ALU.add), 16)
                g.wait_ge(dv.h, dv_b + 54)
                yd.inc(g.dma_start(out=k.d["oz_spill"][ch * 128:(ch + 1) * 128, :],
                                   in_=OZC[:, :]), 16)
                g.wait_ge(yd.h, yd.n)
    k.marks = [sx, pe, ev, dv, po, yd, yds[0], yds[1], svF, bcs[0], bcs[1], ccs[0], ccs[1]]
    fs.close()


def emit_blockG(k: Ker):
    nc = k.nc
    k.fence()
    from contextlib import ExitStack
    gs = ExitStack()
    X = k.s["X"]
    OPS = k.sb("gOPS", [128, 2048], F32, stack=gs)
    OPT = k.sb("gOPT", [128, 16 * 1024], BF16, stack=gs)
    OZG = [k.sb(f"gOZ{i}", [128, T], BF16, stack=gs) for i in range(2)]
    PG = [k.ps(f"gPG{i}", [128, 512], F32, stack=gs) for i in range(8)]
    sx = k.sem("sxG", gs)
    pe = k.sem("peG", gs)
    dvr = k.sem("dvG", gs)
    st = k.sem("stG", gs)
    ozs = [k.sem("ozG0", gs), k.sem("ozG1", gs)]

    cvG = k.sem("cvG", gs)
    sxg = [k.sem("sxGa", gs), k.sem("sxGb", gs)]
    with nc.Block() as blk:
        @blk.sync
        def _(s):
            for ch in range(NT):
                par = ch % 2
                if ch >= 2:
                    s.wait_ge(cvG.h, ch - 1)
                sxg[par].inc(s.dma_start(
                    out=OPS[:, par * 1024:(par + 1) * 1024],
                    in_=k.d["out_projT"][ch * 128:(ch + 1) * 128, :]), 16)

        @blk.scalar
        def _(sc):
            for ch in range(NT):
                par = ch % 2
                sc.wait_ge(sxg[par].h, 16 * (ch // 2 + 1))
                cvG.inc(sc.copy(OPT[:, ch * 1024:(ch + 1) * 1024],
                                OPS[:, par * 1024:(par + 1) * 1024]))

    for tg in range(4):
        with nc.Block() as blk:
            pe_b = pe.n
            oz_b = [ozs[0].n, ozs[1].n]

            @blk.sync
            def _(s):
                for kch in range(NT):
                    par = kch % 2
                    if kch >= 2:
                        s.wait_ge(pe.h, pe_b + kch - 1)
                    ozs[par].inc(s.dma_start(
                        out=OZG[par][:, :],
                        in_=k.d["oz_spill"][kch * 128:(kch + 1) * 128, :]), 16)

            @blk.tensor
            def _(t):
                for kch in range(NT):
                    par = kch % 2
                    t.wait_ge(ozs[par].h, oz_b[par] + 16 * (kch // 2 + 1))
                    last = None
                    for m in range(4):
                        for n in range(2):
                            last = t.matmul(
                                PG[m * 2 + n][:, :],
                                OZG[par][:, tg * 512 + m * 128:
                                         tg * 512 + (m + 1) * 128],
                                OPT[:, kch * 1024 + n * 512:
                                    kch * 1024 + (n + 1) * 512],
                                start=(kch == 0), stop=(kch == NT - 1))
                    pe.inc(last)

            @blk.vector
            def _(v):
                v.wait_ge(pe.h, pe.n)
                for m in range(4):
                    for n in range(2):
                        tile = tg * 4 + m
                        xsl = X[:, tile * 1024 + n * 512: tile * 1024 + (n + 1) * 512]
                        dvr.inc(v.tensor_add(xsl, xsl, PG[m * 2 + n][:, :]))

    with nc.Block() as blk:
        @blk.sync
        def _(s):
            st.inc(s.dma_start(
                out=bass.AP(k.d["out"].tensor, 0,
                            [[1024, 128], [128 * 1024, NT], [1, 1024]]),
                in_=X[:, :]), 16)
            s.wait_ge(st.h, st.n)
    gs.close()
    k.e2.close()
    k.d2.close()


def make_constants():
    c = {}
    N, HDIM = 2048, 64
    pos = np.arange(N, dtype=np.float64)
    inv = 10000.0 ** (-2.0 * np.arange(HDIM // 2) / HDIM)
    ang = pos[:, None] * inv[None, :]
    cosr = np.cos(ang)
    sinr = np.sin(ang)
    cos2 = np.zeros((128, N), dtype=np.float64)
    sinf2 = np.zeros((128, N), dtype=np.float64)
    for j in range(128):
        jd = j % 64
        cos2[j] = cosr[:, jd // 2]
        sinf2[j] = sinr[:, jd // 2] * (-1.0 if jd % 2 == 0 else 1.0)
    c["cos2"] = cos2.astype(ml_dtypes.bfloat16)
    c["sinf2"] = sinf2.astype(ml_dtypes.bfloat16)
    pswap = np.zeros((128, 128), dtype=np.float64)
    for j in range(128):
        pswap[j ^ 1, j] = 1.0
    c["pswap"] = pswap.astype(ml_dtypes.bfloat16)
    tri = (np.arange(128)[:, None] <= np.arange(128)[None, :]).astype(np.float32)
    c["tri"] = tri
    c["ident128f"] = np.eye(128, dtype=np.float32)
    c["ident128b"] = np.eye(128).astype(ml_dtypes.bfloat16)
    c["ident16f"] = np.eye(16, dtype=np.float32)
    c["ident65f"] = np.eye(65, dtype=np.float32)
    return c


def tr(a):
    return np.ascontiguousarray(np.asarray(a).T)


def make_in_maps(inputs, n_cores=8):
    consts = make_constants()
    x = np.asarray(inputs["L0_attn_x"])
    shared = dict(consts)
    shared["WqT"] = tr(inputs["L0_attn_Wq"])
    shared["WkT"] = tr(inputs["L0_attn_Wk"])
    shared["WvT"] = tr(inputs["L0_attn_Wv"])
    shared["gate_wT"] = tr(inputs["L0_ffn_gate_w"])
    shared["w1"] = np.asarray(inputs["L0_ffn_w1"])
    shared["w2"] = np.asarray(inputs["L0_ffn_w2"])
    shared["in_projT"] = tr(inputs["L1_attn_in_proj_w"])
    shared["conv_w"] = np.asarray(inputs["L1_attn_conv1d_w"])
    shared["conv_b"] = np.asarray(inputs["L1_attn_conv1d_b"])[None, :]
    xp = tr(inputs["L1_attn_x_proj_w"])
    xpp = np.zeros((2048, 128), dtype=np.float32)
    xpp[:, 0:16] = xp[:, 80:96]
    xpp[:, 32:48] = xp[:, 64:80]
    xpp[:, 64:128] = xp[:, 0:64]
    shared["x_projT"] = xpp
    shared["dt_projT"] = tr(inputs["L1_attn_dt_proj_w"])
    shared["dt_proj_b"] = np.asarray(inputs["L1_attn_dt_proj_b"])[None, :]
    shared["A_log"] = np.asarray(inputs["L1_attn_A_log"])
    shared["D_param"] = np.asarray(inputs["L1_attn_D_param"])[None, :]
    shared["out_projT"] = tr(inputs["L1_attn_out_proj_w"])
    shared["l1w"] = np.asarray(inputs["L1_ffn_weight"])
    shared["l1b"] = np.asarray(inputs["L1_ffn_bias"])[None, :]
    maps = []
    for r in range(n_cores):
        m = dict(shared)
        m["x"] = np.ascontiguousarray(x[r])
        m["rank"] = np.array([[float(r)]], dtype=np.float32)
        maps.append(m)
    return maps


from contextlib import ExitStack
from concourse.bass_utils import run_bass_kernel_spmd


def _build_nc():
    nc = bass.Bass("TRN2", target_bir_lowering=False)
    with ExitStack() as stack:
        k = Ker(nc, stack, taps=())
        declare_inputs(k)
        emit_blockA(k)
        emit_blockB(k)
        emit_blockC(k)
        emit_blockD(k)
        emit_blockE(k)
        emit_blockF(k)
        emit_blockG(k)
    return nc


def kernel(**inputs):
    maps = make_in_maps(inputs, n_cores=8)
    nc = _build_nc()
    res = run_bass_kernel_spmd(nc, maps, core_ids=list(range(8)))
    out = np.stack([np.asarray(res.results[b]["out"], dtype=np.float32)
                    for b in range(8)], axis=0)
    return out
